# version 04
# baseline (speedup 1.0000x reference)
import math
import numpy as np
import ml_dtypes
import contextlib
import concourse.bass as bass
import concourse.mybir as mybir
from concourse.bass_utils import run_bass_kernel_spmd

F32 = mybir.dt.float32
BF16 = mybir.dt.bfloat16
I32 = mybir.dt.int32
AF = mybir.ActivationFunctionType
ALU = mybir.AluOpType
AX = mybir.AxisListType

NCORES = 8
SEQ = 2048
DM = 1024
NSEQ = 2
TOK = NSEQ * SEQ
NTILE = TOK // 128
TPS = SEQ // 128
NG = TPS // 4
NH, HD = 16, 64
NEXP, DFF = 16, 4096
CLOC = 384
CAP = 2560
TRASH_R = NEXP * CLOC
TRASH_W = TRASH_R + 1
EPS = 1e-5

CF_EPS = 0
CF_NEG2 = 1
CF_ONES = 2
CF_SUT = 130
CF_IDF = 258
CF_IOTA = 386
CF_REV = 402
CF_BIAS = 418
CB_ID = 0
CB_P64 = 128
CB_P32 = 256
CB_UT = 384
CB_ONES = 512
CB_UTBIG = 576
CB_COLS = 704

_BIAS_ORDER = [
    ("bq", 256), ("bqc", 896), ("bqr", 128), ("bkv", 256), ("bkr", 128),
    ("bkc", 896), ("bv", 1024), ("bo", 1024), ("bqkv", 3072), ("bproj", 1024),
]


def _rope_tables():
    t = np.arange(SEQ, dtype=np.float64)
    fa = 10000.0 ** (-(np.arange(64) / 64.0))
    ca = np.cos(t[None, :] * fa[:, None])
    sa = np.sin(t[None, :] * fa[:, None])
    C1a = np.concatenate([ca, ca], 0)
    C2a = np.concatenate([-sa, sa], 0)
    fb = 10000.0 ** (-2.0 * np.arange(32) / 64.0)
    cb = np.cos(t[None, :] * fb[:, None])
    sb = np.sin(t[None, :] * fb[:, None])
    C1b = np.concatenate([cb, cb, cb, cb], 0)
    C2b = np.concatenate([-sb, sb, -sb, sb], 0)
    return (C1a.astype(ml_dtypes.bfloat16), C2a.astype(ml_dtypes.bfloat16),
            C1b.astype(ml_dtypes.bfloat16), C2b.astype(ml_dtypes.bfloat16))


def _perm_qk_cols():
    per_head = np.concatenate([np.arange(0, 64, 2), np.arange(1, 64, 2)])
    p = np.concatenate([h * 64 + per_head for h in range(NH)])
    return np.concatenate([p, 1024 + p, np.arange(2048, 3072)])


def _build_consts(bias_map):
    total_bias = sum(len(v) // 128 for _, v in bias_map)
    ncols = CF_BIAS + total_bias
    cst = np.zeros((128, ncols), dtype=np.float32)
    cst[:, CF_EPS] = EPS
    cst[:, CF_NEG2] = -2.0
    cst[:, CF_ONES:CF_ONES + 128] = 1.0
    cst[:, CF_SUT:CF_SUT + 128] = np.triu(np.ones((128, 128), dtype=np.float32), k=1)
    cst[:, CF_IDF:CF_IDF + 128] = np.eye(128, dtype=np.float32)
    cst[:, CF_IOTA:CF_IOTA + 16] = np.arange(16, dtype=np.float32)[None, :]
    cst[:, CF_REV:CF_REV + 16] = 99.0 - np.arange(16, dtype=np.float32)[None, :]
    off = CF_BIAS
    for _, vec in bias_map:
        nck = len(vec) // 128
        cst[:, off:off + nck] = vec.reshape(nck, 128).T
        off += nck
    cstb = np.zeros((128, CB_COLS), dtype=np.float32)
    cstb[:, CB_ID:CB_ID + 128] = np.eye(128)
    p64 = np.zeros((128, 128)); p32 = np.zeros((128, 128))
    for i in range(128):
        p64[(i + 64) % 128, i] = 1.0
        p32[i ^ 32, i] = 1.0
    cstb[:, CB_P64:CB_P64 + 128] = p64
    cstb[:, CB_P32:CB_P32 + 128] = p32
    cstb[:, CB_UT:CB_UT + 128] = np.triu(np.ones((128, 128)))
    cstb[:, CB_ONES:CB_ONES + 64] = 1.0
    ut = np.zeros((128, 128)); ut[np.tril_indices(128, -1)] = -1e9
    cstb[:, CB_UTBIG:CB_UTBIG + 128] = ut
    return cst, cstb.astype(ml_dtypes.bfloat16)


SIM_NOCC = False

def build_program(n_sub=4):
    nc = bass.Bass("TRN2", target_bir_lowering=False, debug=False)

    xin = nc.dram_tensor("xin", [TOK, DM], F32, kind="ExternalInput").ap()
    out = nc.dram_tensor("out", [TOK, DM], F32, kind="ExternalOutput").ap()
    wdr = {}
    for nm, shp in [
        ("wq", [1024, 256]), ("wqc", [256, 896]), ("wqr", [1024, 128]),
        ("wkv", [1024, 256]), ("wkr", [1024, 128]), ("wkc", [256, 896]),
        ("wv", [256, 1024]), ("wo", [1024, 1024]),
        ("wqkv", [1024, 3072]), ("wproj", [1024, 1024]),
        ("gw", [16, 1024]), ("w1", [16, 1024, 4096]), ("w2", [16, 4096, 1024]),
    ]:
        wdr[nm] = nc.dram_tensor(nm, shp, F32, kind="ExternalInput").ap()
    total_bias = sum(n // 128 for _, n in _BIAS_ORDER)
    CSTF_COLS = CF_BIAS + total_bias
    cstf_dr = nc.dram_tensor("cstf", [128, CSTF_COLS], F32, kind="ExternalInput").ap()
    cstb_dr = nc.dram_tensor("cstb", [128, CB_COLS], BF16, kind="ExternalInput").ap()
    rope_dr = nc.dram_tensor("rope", [128, 4 * SEQ], BF16, kind="ExternalInput").ap()
    cmask_dr = nc.dram_tensor("cmask", [8, 1], F32, kind="ExternalInput").ap()

    xres = nc.dram_tensor("xres", [TOK, DM], F32, kind="Internal").ap()
    disp = nc.dram_tensor("disp", [TRASH_W + 1, DM], BF16, kind="Internal").ap()
    expout = nc.dram_tensor("expout", [TRASH_R + 1, DM], BF16, kind="Internal").ap()
    ccin = nc.dram_tensor("ccin", [1, 512], F32, kind="Internal")
    ccout = nc.dram_tensor("ccout", [8, 512], F32, kind="Internal")

    bias_cols = {}
    off = CF_BIAS
    for nm, n in _BIAS_ORDER:
        bias_cols[nm] = off
        off += n // 128

    cstf = nc.alloc_sbuf_tensor("cstf_sb", [128, CSTF_COLS], F32)
    cstb = nc.alloc_sbuf_tensor("cstb_sb", [128, CB_COLS], BF16)
    xsl = [nc.alloc_sbuf_tensor(f"xsl{i}", [128, DM], F32) for i in range(4)]
    xc = nc.alloc_sbuf_tensor("xc", [128, DM], F32)
    lnb = [nc.alloc_sbuf_tensor(f"lnb{i}", [128, DM], BF16) for i in range(4)]
    cols = nc.alloc_sbuf_tensor("cols", [128, 16], F32)
    lrow = nc.alloc_sbuf_tensor("lrow", [1, 512], F32)
    rrow = nc.alloc_sbuf_tensor("rrow", [1, 512], F32)
    rl_hi = nc.alloc_sbuf_tensor("rl_hi", [1, 512], BF16)
    rl_lo = nc.alloc_sbuf_tensor("rl_lo", [1, 512], BF16)
    gwT = nc.alloc_sbuf_tensor("gwT", [128, 128], F32)
    cmask = nc.alloc_sbuf_tensor("cmask_sb", [8, 1], F32)
    sg8 = nc.alloc_sbuf_tensor("sg8", [8, 16], F32)
    gbase = nc.alloc_sbuf_tensor("gbase", [1, 16], F32)
    gbb = nc.alloc_sbuf_tensor("gbb", [128, 16], F32)
    runoff = nc.alloc_sbuf_tensor("runoff", [1, 16], F32)
    idxb = nc.alloc_sbuf_tensor("idxb", [128, NTILE], I32)
    probb = nc.alloc_sbuf_tensor("probb", [128, NTILE], F32)
    rankb = nc.alloc_sbuf_tensor("rankb", [128, NTILE], F32)
    eidxb = nc.alloc_sbuf_tensor("eidxb", [128, NTILE], F32)
    wcomb = nc.alloc_sbuf_tensor("wcomb", [128, NTILE], F32)
    arena = nc.alloc_sbuf_tensor("warena", [128, 16384], BF16)

    PF = [nc.alloc_psum_tensor(f"pf{i}", [128, 512], F32) for i in range(7)]
    TB = nc.alloc_psum_tensor("tbp", [128, 1024], BF16)

    sem_names = (["pe", "act", "dve", "pool", "syn", "cc", "dcst", "dwgt",
                     "ddsp", "deot", "dcc", "dw10", "dw11", "dw12", "dw20", "dw21",
                     "dwa0", "dwa1", "dwa2", "dwa3"]
                    + [f"dx{i}" for i in range(4)]
                    + [f"dsct{i}" for i in range(4)]
                    + [f"dgx{i}" for i in range(4)])
    ENGS = ["tensor", "scalar", "vector", "gpsimd", "sync"]

    with contextlib.ExitStack() as topstack:
        SEM = {nm: topstack.enter_context(nc.semaphore(nm)) for nm in sem_names}
        C = {nm: 0 for nm in sem_names}
        SATIS = {e: {nm: 0 for nm in sem_names} for e in ENGS}
        blk = topstack.enter_context(nc.Block())

        REC = {e: [] for e in ENGS}
        CURSCOPE = ["top"]

        class _RInstr:
            __slots__ = ("incs",)
            def __init__(self):
                self.incs = []
            def then_inc(self, sem, n=1):
                self.incs.append((sem, n))
                return self

        class _REng:
            __slots__ = ("ops",)
            def __init__(self, ops):
                self.ops = ops
            def __getattr__(self, name):
                ops = self.ops
                def call(*a, **k):
                    ri = _RInstr()
                    ops.append((name, a, k, ri, CURSCOPE[0]))
                    return ri
                return call

        def on(eng, fn):
            fn(_REng(REC[eng]))

        def flush_sections():
            for eng in ENGS:
                ops = REC[eng]
                if not ops:
                    continue
                def body(e, ops=ops):
                    cur = None
                    sid = None
                    for name, a, k, ri, scope in ops:
                        if scope != cur:
                            if cur is not None:
                                nc.leave_named_scope(cur, sid, False)
                            sid, _ = nc.enter_named_scope(scope, False)
                            cur = scope
                        instr = getattr(e, name)(*a, **k)
                        for sem, n in ri.incs:
                            instr.then_inc(sem, n)
                    if cur is not None:
                        nc.leave_named_scope(cur, sid, False)
                getattr(blk, eng)(body)

        def wt(e, eng, *reqs):
            for r in reqs:
                if r is None:
                    continue
                nm, ct = r
                if ct <= 0 or SATIS[eng][nm] >= ct:
                    continue
                e.wait_ge(SEM[nm], ct)
                SATIS[eng][nm] = ct

        def fin(instr, nm, n=1):
            C[nm] += n
            instr.then_inc(SEM[nm], n)
            return C[nm]

        def barrier():
            for eng, nm in [("tensor", "pe"), ("scalar", "act"), ("vector", "dve"),
                            ("gpsimd", "pool"), ("sync", "syn")]:
                def f(e, nm=nm):
                    fin(e.drain(), nm)
                on(eng, f)
            snap = dict(C)
            for eng in ENGS:
                def f(e, eng=eng, snap=snap):
                    wt(e, eng, *snap.items())
                on(eng, f)

        ST = {
            "tb_r": None,
            "tb_resid": 0,
            "xc_act": 0,
        }
        for i in range(4):
            ST[f"xsl{i}"] = None
            ST[f"lnb{i}"] = None

        gwraw_cm = nc.sbuf_tensor("gwraw", [16, 1024], F32)
        gwraw = gwraw_cm.__enter__()
        def f(e):
            fin(e.dma_start(out=cstf[:], in_=cstf_dr), "dcst", 16)
            fin(e.dma_start(out=cstb[:], in_=cstb_dr), "dcst", 16)
            fin(e.dma_start(out=cmask[:], in_=cmask_dr), "dcst", 16)
            fin(e.dma_start(out=gwraw[:], in_=wdr["gw"]), "dcst", 16)
        on("sync", f)
        start_dsy = C["dcst"]

        def f(e):
            wt(e, "tensor", ("dcst", start_dsy))
            for c in range(8):
                ins = e.transpose(PF[0][:, c * 16:(c + 1) * 16], gwraw[:, c * 128:(c + 1) * 128],
                                  cstf[0:16, CF_IDF:CF_IDF + 16])
            fin(ins, "pe")
        on("tensor", f)
        gw_pe = C["pe"]
        def f(e):
            wt(e, "scalar", ("pe", gw_pe))
            fin(e.activation(gwT[:], PF[0][:, 0:128], AF.Copy), "act")
        on("scalar", f)
        gwT_act = C["act"]
        def f(e):
            fin(e.memset(lnb[0][0:1, :], 0.0), "dve")
        on("vector", f)
        z_dve = C["dve"]
        def f(e):
            wt(e, "sync", ("dve", z_dve), ("dcst", C["dcst"]))
            fin(e.dma_start(out=expout[TRASH_R:TRASH_R + 1, :], in_=lnb[0][0:1, :]), "dcst", 16)
        on("sync", f)
        barrier()
        gwraw_cm.__exit__(None, None, None)

        def emit_ln_one(src_dram, tj, i, lnf_slot, lnb_slots, xslots):
            def f(e):
                wt(e, "sync", ST[f"xsl{i}"], (f"dx{i}", C[f"dx{i}"]))
                fin(e.dma_start(out=xslots[i][:],
                                in_=src_dram[tj * 128:(tj + 1) * 128, :]), f"dx{i}", 16)
            on("sync", f)
            load = (f"dx{i}", C[f"dx{i}"])
            def fv(e, i=i):
                wt(e, "vector", load, ("act", ST["xc_act"]))
                fin(e.reduce_sum(cols[:, 0:1], xslots[i][:], axis=AX.X), "dve")
                wt(e, "vector", ("dve", C["dve"]))
                fin(e.tensor_scalar(cols[:, 1:2], cols[:, 0:1], 1.0 / DM, None, ALU.mult), "dve")
                wt(e, "vector", ("dve", C["dve"]))
                fin(e.tensor_scalar(xc[:], xslots[i][:], cols[:, 1:2], None, ALU.subtract), "dve")
            on("vector", fv)
            xc_dve = C["dve"]
            def fs(e, i=i):
                wt(e, "scalar", ("dve", xc_dve), ST[f"lnb{i}"])
                fin(e.activation(lnb_slots[i][:], xc[:], AF.Square, accum_out=cols[:, 2:3]), "act")
                wt(e, "scalar", ("act", C["act"]))
                fin(e.activation(cols[:, 3:4], cols[:, 2:3], AF.Ln,
                                 bias=cstf[:, CF_EPS:CF_EPS + 1], scale=1.0 / DM), "act")
                wt(e, "scalar", ("act", C["act"]))
                fin(e.activation(cols[:, 4:5], cols[:, 3:4], AF.Exp, scale=-0.5), "act")
            on("scalar", fs)
            sd_act = C["act"]
            ST["xc_act"] = sd_act
            def fv2(e, i=i):
                wt(e, "vector", ("act", sd_act))
                fin(e.tensor_scalar(lnb_slots[i][:], xc[:], cols[:, 4:5], None, ALU.mult), "dve")
                if lnf_slot is not None:
                    wt(e, "vector", ST.get("lnf_war"))
                    fin(e.tensor_scalar(lnf_slot[:], xc[:], cols[:, 4:5], None, ALU.mult), "dve")
            on("vector", fv2)
            return C["dve"], load

        def emit_ln(src_dram, tiles, lnf_slots, lnb_slots, xslots):
            res = []; loads = []
            for i, tj in enumerate(tiles):
                d, l = emit_ln_one(src_dram, tj, i, None, lnb_slots, xslots)
                res.append(d); loads.append(l)
            return res, loads

        def emit_attn(flavor, src_dram, dst_dram):
            sfx = "_" + flavor
            sc = contextlib.ExitStack()
            ropes = sc.enter_context(nc.sbuf_tensor("ropesh" + sfx, [128, 2 * SEQ], BF16))
            Kb = [sc.enter_context(nc.sbuf_tensor(f"Kb{c}" + sfx, [128, SEQ], BF16)) for c in range(8)]
            V65 = sc.enter_context(nc.sbuf_tensor("V65" + sfx, [128, TPS * 1040], BF16))
            Qb = [sc.enter_context(nc.sbuf_tensor(f"Qb{c}" + sfx, [128, 512], BF16)) for c in range(8)]
            lnT = [sc.enter_context(nc.sbuf_tensor(f"lnT{c}" + sfx, [128, 512], BF16)) for c in range(8)]
            OT = lnT
            AT = [sc.enter_context(nc.sbuf_tensor(f"AT{i}" + sfx, [128, 512], BF16)) for i in range(4)]
            pre = sc.enter_context(nc.sbuf_tensor("pre" + sfx, [128, 512], BF16))
            rs1 = sc.enter_context(nc.sbuf_tensor("rs1" + sfx, [128, 512], BF16))
            rs2 = sc.enter_context(nc.sbuf_tensor("rs2" + sfx, [128, 512], BF16))
            rbc = sc.enter_context(nc.sbuf_tensor("rbc" + sfx, [64, 512], F32))
            vT = sc.enter_context(nc.sbuf_tensor("vT" + sfx, [128, 512], BF16))
            WS = {}
            if flavor == "a":
                cqT = [sc.enter_context(nc.sbuf_tensor(f"cqT{i}" + sfx, [128, 512], BF16)) for i in range(2)]
                ckvT = [sc.enter_context(nc.sbuf_tensor(f"ckvT{i}" + sfx, [128, 512], BF16)) for i in range(2)]
                for nm, base, stride, kch in [("wq", 0, 256, 8), ("wkv", 2048, 256, 8),
                                              ("wqc", 4096, 896, 2), ("wkc", 5888, 896, 2),
                                              ("wqr", 7680, 128, 8), ("wkr", 8704, 128, 8),
                                              ("wv", 9728, 1024, 2)]:
                    WS[nm] = (arena, base, stride, kch)
                swo = sc.enter_context(nc.sbuf_tensor("swo" + sfx, [128, 8 * 1024], BF16))
                WS["wo"] = (swo, 0, 1024, 8)
            else:
                WS["wqkv_qk"] = (arena, 0, 2048, 8)
                svth = sc.enter_context(nc.sbuf_tensor("svth" + sfx, [128, 8 * 1024], BF16))
                WS["wqkv_v"] = (svth, 0, 1024, 8)
                swp = sc.enter_context(nc.sbuf_tensor("swp" + sfx, [128, 8 * 1024], BF16))
                WS["wproj"] = (swp, 0, 1024, 8)
            def fr(e):
                roff = (0 if flavor == "a" else 2) * SEQ
                fin(e.dma_start(out=ropes[:], in_=rope_dr[:, roff:roff + 2 * SEQ]), "dwa3", 16)
            on("sync", fr)
            rope_rdy = ("dwa3", C["dwa3"])
            arena_rdy = ST.get("arena_rdy", ("dwa0", 0))
            if flavor == "a":
                def fw(e):
                    s2 = wdr["wo"].rearrange("(kc p) n -> p kc n", p=128)
                    fin(e.dma_start(out=swo[:].rearrange("p (kc n) -> p kc n", kc=8), in_=s2), "dwgt", 16)
                on("gpsimd", fw)
                wrdy = {nm: arena_rdy for nm in
                        ["wq", "wkv", "wqc", "wkc", "wqr", "wkr", "wv"]}
            else:
                def fw(e):
                    s2 = wdr["wqkv"].rearrange("(kc p) n -> p kc n", p=128)
                    fin(e.dma_start(out=svth[:].rearrange("p (kc n) -> p kc n", kc=8),
                                    in_=s2[:, :, 2048:3072]), "dwa2", 16)
                    s3 = wdr["wproj"].rearrange("(kc p) n -> p kc n", p=128)
                    fin(e.dma_start(out=swp[:].rearrange("p (kc n) -> p kc n", kc=8), in_=s3), "dwgt", 16)
                on("gpsimd", fw)
                wrdy = {"q": arena_rdy, "k": arena_rdy,
                        "v": ("dwa2", C["dwa2"]), "wproj": ("dwgt", C["dwgt"])}
            wgt_rdy = ("dwgt", C["dwgt"])

            rope_off = 0
            def fv65(e):
                v = V65[:].rearrange("p (t h c) -> p t h c", t=TPS, h=NH)
                fin(e.memset(v[:, :, :, 64:65], 1.0), "dve")
            on("vector", fv65)
            v65_ones = ("dve", C["dve"])
            Pcol = CB_P64 if flavor == "a" else CB_P32

            psum_r = {i: None for i in range(7)}
            psum_w = {i: 0 for i in range(7)}
            LS = {"pre_dve": 0, "rs1_act": 0}

            def mm_group(e, pslot, w, kc_list, wcol, rhs_list, n=512):
                wt_, base, stride, _ = WS[w]
                for idx, kc in enumerate(kc_list):
                    off = base + kc * stride + wcol
                    ins = e.matmul(pslot[:, 0:n], wt_[:, off:off + 128],
                                   rhs_list[idx][:, 0:n],
                                   start=(idx == 0), stop=(idx == len(kc_list) - 1))
                return fin(ins, "pe")

            def proj(wname, mc, rhs_list, kc_list, dst_ap, func, bias_col, pslot,
                     rhs_ready, dst_war=None, w_rdy=None):
                def ft(e):
                    wt(e, "tensor", rhs_ready, psum_r[pslot], w_rdy or wgt_rdy)
                    mm_group(e, PF[pslot], wname, kc_list, mc * 128, rhs_list)
                on("tensor", ft)
                psum_w[pslot] = C["pe"]
                def fs(e):
                    wt(e, "scalar", ("pe", psum_w[pslot]), dst_war)
                    f2 = AF.Identity if func == AF.Copy else func
                    fin(e.activation(dst_ap, PF[pslot][:, 0:512], f2,
                                     bias=cstf[:, bias_col:bias_col + 1]), "act")
                on("scalar", fs)
                psum_r[pslot] = ("act", C["act"])
                return C["act"]

            def rope_chunk(src_sb, dst_ap, pos0, gelu):
                src_act = C["act"]
                def ft(e):
                    wt(e, "tensor", ("act", src_act), psum_r[4])
                    fin(e.matmul(PF[4][:, 0:512], cstb[:, Pcol:Pcol + 128], src_sb[:, 0:512],
                                 start=True, stop=True), "pe")
                on("tensor", ft)
                psum_w[4] = C["pe"]
                def fv(e):
                    wt(e, "vector", ("act", src_act), ("pe", psum_w[4]),
                       ("act", LS["rs1_act"]), rope_rdy)
                    e.tensor_tensor(rs1[:], src_sb[:],
                                    ropes[:, rope_off + pos0:rope_off + pos0 + 512], ALU.mult)
                    fin(e.tensor_tensor(rs2[:], PF[4][:, 0:512],
                                        ropes[:, rope_off + SEQ + pos0:rope_off + SEQ + pos0 + 512], ALU.mult), "dve", 2)
                    wt(e, "vector", ("dve", C["dve"]))
                    fin(e.tensor_tensor(rs1[:] if gelu else dst_ap, rs1[:], rs2[:], ALU.add), "dve")
                on("vector", fv)
                psum_r[4] = ("dve", C["dve"])
                LS["pre_dve"] = C["dve"]
                if gelu:
                    def fs(e):
                        wt(e, "scalar", ("dve", C["dve"]))
                        fin(e.activation(dst_ap, rs1[:], AF.Gelu), "act")
                    on("scalar", fs)
                    LS["rs1_act"] = C["act"]

            def v_to_V65(g, mv, vact):
                for j in range(4):
                    def ft(e, j=j, vact=vact):
                        wt(e, "tensor", ("act", vact), ST["tb_r"])
                        fin(e.transpose(TB[:, 0:128], vT[:, j * 128:(j + 1) * 128],
                                        cstb[:, CB_ID:CB_ID + 128]), "pe")
                    on("tensor", ft)
                    tbw = C["pe"]
                    def fe(e, j=j, mv=mv, tbw=tbw):
                        wt(e, "scalar", ("pe", tbw))
                        tbase = (4 * g + j) * 1040
                        hA, hB = 2 * mv, 2 * mv + 1
                        e.activation(V65[:, tbase + 65 * hA:tbase + 65 * hA + 64], TB[:, 0:64], AF.Copy)
                        fin(e.activation(V65[:, tbase + 65 * hB:tbase + 65 * hB + 64], TB[:, 64:128], AF.Copy), "act")
                    on("scalar", fe)
                    ST["tb_r"] = ("act", C["act"])

            for s in range(NSEQ):
                seq_t0 = s * TPS
                for g in range(NG):
                    tiles = [seq_t0 + 4 * g + j for j in range(4)]
                    pos0 = 512 * g
                    attn_pe_prev = ST.get("attn_pe_done", 0)
                    lndve, _ = emit_ln(src_dram, tiles, None, lnb, xsl)
                    for j in range(4):
                        def ft(e, j=j):
                            wt(e, "tensor", ("dve", lndve[j]), ST["tb_r"])
                            for cch in range(8):
                                ins = e.transpose(TB[:, cch * 128:(cch + 1) * 128],
                                                  lnb[j][:, cch * 128:(cch + 1) * 128],
                                                  cstb[:, CB_ID:CB_ID + 128])
                            fin(ins, "pe")
                        on("tensor", ft)
                        tbw = C["pe"]
                        ST[f"lnb{j}"] = ("pe", tbw)
                        def fe(e, j=j, tbw=tbw):
                            wt(e, "scalar", ("pe", tbw))
                            for cch in range(8):
                                ins = e.activation(lnT[cch][:, j * 128:(j + 1) * 128],
                                                   TB[:, cch * 128:(cch + 1) * 128], AF.Copy)
                            fin(ins, "act")
                        on("scalar", fe)
                        ST["tb_r"] = ("act", C["act"])
                    lnT_ready = ("act", C["act"])

                    if flavor == "a":
                        for i in range(2):
                            proj("wq", i, lnT, list(range(8)), cqT[i][:], AF.Gelu,
                                 bias_cols["bq"] + i, i % 2, lnT_ready, w_rdy=wrdy["wq"])
                        for i in range(2):
                            proj("wkv", i, lnT, list(range(8)), ckvT[i][:], AF.Gelu,
                                 bias_cols["bkv"] + i, i % 2, lnT_ready, w_rdy=wrdy["wkv"])
                        cq_ready = ("act", C["act"])
                        for mcq in range(7):
                            proj("wqc", mcq, cqT, [0, 1], Qb[mcq][:], AF.Gelu,
                                 bias_cols["bqc"] + mcq, mcq % 2, cq_ready, w_rdy=wrdy["wqc"])
                        proj("wqr", 0, lnT, list(range(8)), pre[:], AF.Copy,
                             bias_cols["bqr"], 2, lnT_ready, dst_war=("dve", LS["pre_dve"]), w_rdy=wrdy["wqr"])
                        rope_chunk(pre, Qb[7][:], pos0, gelu=True)
                        for mck in range(7):
                            proj("wkc", mck, ckvT, [0, 1], Kb[mck][:, pos0:pos0 + 512], AF.Gelu,
                                 bias_cols["bkc"] + mck, mck % 2, cq_ready, w_rdy=wrdy["wkc"])
                        proj("wkr", 0, lnT, list(range(8)), pre[:], AF.Copy,
                             bias_cols["bkr"], 2, lnT_ready, dst_war=("dve", LS["pre_dve"]), w_rdy=wrdy["wkr"])
                        rope_chunk(pre, Kb[7][:, pos0:pos0 + 512], pos0, gelu=False)
                        for mv in range(8):
                            vact = proj("wv", mv, ckvT, [0, 1], vT[:], AF.Copy,
                                        bias_cols["bv"] + mv, 3, cq_ready, w_rdy=wrdy["wv"])
                            v_to_V65(g, mv, vact)
                    else:
                        for mq in range(8):
                            proj("wqkv_qk", mq, lnT, list(range(8)), pre[:], AF.Copy,
                                 bias_cols["bqkv"] + mq, mq % 2, lnT_ready,
                                 dst_war=("dve", LS["pre_dve"]), w_rdy=wrdy["q"])
                            rope_chunk(pre, Qb[mq][:], pos0, gelu=False)
                        for mk in range(8):
                            proj("wqkv_qk", 8 + mk, lnT, list(range(8)), pre[:], AF.Copy,
                                 bias_cols["bqkv"] + 8 + mk, mk % 2, lnT_ready,
                                 dst_war=("dve", LS["pre_dve"]), w_rdy=wrdy["k"])
                            rope_chunk(pre, Kb[mk][:, pos0:pos0 + 512], pos0, gelu=False)
                        for mv in range(8):
                            vact = proj("wqkv_v", mv, lnT, list(range(8)), vT[:], AF.Copy,
                                        bias_cols["bqkv"] + 16 + mv, 3, lnT_ready, w_rdy=wrdy["v"])
                            v_to_V65(g, mv, vact)

                    qkv_ready = ("act", C["act"])
                    qkv_dve = ("dve", C["dve"])

                    nkt = 4 * g + 4
                    pair_exp_done = C["act"]
                    pav_r = ("dve", 0)
                    for c in range(8):
                        st_exp = {}
                        av_grp = {}
                        mm_grp = {}

                        def emit_av(kt, c, pav_r):
                            slot = kt % 2
                            qs = max(0, (kt - 4 * g)) * 128 if kt >= 4 * g else 0
                            def fa(e, kt=kt, c=c, slot=slot, qs=qs, pav_r=pav_r):
                                wt(e, "tensor", ("act", st_exp[kt]))
                                if kt == 0:
                                    wt(e, "tensor", pav_r, v65_ones)
                                tbase = kt * 1040
                                e.matmul(PF[5][0:65, qs:512],
                                         V65[:, tbase + 65 * (2 * c):tbase + 65 * (2 * c) + 65],
                                         AT[slot][:, qs:512], start=(kt == 0), stop=(kt == nkt - 1))
                                fin(e.matmul(PF[6][0:65, qs:512],
                                             V65[:, tbase + 65 * (2 * c + 1):tbase + 65 * (2 * c + 1) + 65],
                                             AT[2 + slot][:, qs:512], start=(kt == 0), stop=(kt == nkt - 1)), "pe")
                            on("tensor", fa)
                            av_grp[kt] = C["pe"]

                        for kt in range(nkt):
                            slot = kt % 2
                            pA, pB = PF[slot], PF[2 + slot]
                            diag = kt >= 4 * g
                            qs = (kt - 4 * g) * 128 if diag else 0
                            def ft(e, kt=kt, c=c, pA=pA, pB=pB, diag=diag, qs=qs,
                                   pair_exp_done=pair_exp_done):
                                wt(e, "tensor", qkv_ready, qkv_dve)
                                if kt - 2 in st_exp:
                                    wt(e, "tensor", ("act", st_exp[kt - 2]))
                                else:
                                    wt(e, "tensor", ("act", pair_exp_done))
                                e.matmul(pA[:, 0:512], Kb[c][0:64, kt * 128:(kt + 1) * 128],
                                         Qb[c][0:64, :], start=True, stop=not diag,
                                         tile_position=(0, 0))
                                if diag:
                                    e.matmul(pA[:, qs:qs + 128], cstb[:, CB_ID:CB_ID + 128],
                                             cstb[:, CB_UTBIG:CB_UTBIG + 128], start=False, stop=True)
                                ins = e.matmul(pB[:, 0:512], Kb[c][64:128, kt * 128:(kt + 1) * 128],
                                               Qb[c][64:128, :], start=True, stop=not diag,
                                               tile_position=(64, 0))
                                if diag:
                                    ins = e.matmul(pB[:, qs:qs + 128], cstb[:, CB_ID:CB_ID + 128],
                                                   cstb[:, CB_UTBIG:CB_UTBIG + 128], start=False, stop=True)
                                fin(ins, "pe")
                            on("tensor", ft)
                            mm_grp[kt] = C["pe"]
                            if kt - 1 in st_exp and kt - 1 not in av_grp:
                                emit_av(kt - 1, c, pav_r)
                            def fs(e, kt=kt, slot=slot, pA=pA, pB=pB):
                                wt(e, "scalar", ("pe", mm_grp[kt]))
                                if kt - 2 in av_grp:
                                    wt(e, "scalar", ("pe", av_grp[kt - 2]))
                                e.activation(AT[slot][:], pA[:, 0:512], AF.Exp,
                                             bias=cstf[:, CF_NEG2:CF_NEG2 + 1], scale=0.125)
                                fin(e.activation(AT[2 + slot][:], pB[:, 0:512], AF.Exp,
                                                 bias=cstf[:, CF_NEG2:CF_NEG2 + 1], scale=0.125), "act")
                            on("scalar", fs)
                            st_exp[kt] = C["act"]
                        emit_av(nkt - 1, c, pav_r)
                        av_done = C["pe"]
                        pair_exp_done = st_exp[nkt - 1]
                        for hh, pav in ((2 * c, PF[5]), (2 * c + 1, PF[6])):
                            def fs(e, pav=pav):
                                wt(e, "scalar", ("pe", av_done))
                                fin(e.activation(lrow[:], pav[64:65, 0:512], AF.Ln), "act")
                                wt(e, "scalar", ("act", C["act"]))
                                fin(e.activation(rrow[:], lrow[:], AF.Exp, scale=-1.0), "act")
                            on("scalar", fs)
                            lact = C["act"]
                            def fv(e, lact=lact):
                                wt(e, "vector", ("act", lact))
                                fin(e.tensor_copy(rl_hi[:], rrow[:]), "dve")
                                wt(e, "vector", ("dve", C["dve"]))
                                fin(e.tensor_tensor(rl_lo[:], rrow[:], rl_hi[:], ALU.subtract), "dve")
                            on("vector", fv)
                            rl_dve = C["dve"]
                            def ft(e, rl_dve=rl_dve):
                                wt(e, "tensor", ("dve", rl_dve), psum_r[4])
                                e.matmul(PF[4][0:64, 0:512], cstb[0:1, CB_ONES:CB_ONES + 64],
                                         rl_hi[:], start=True, stop=False)
                                fin(e.matmul(PF[4][0:64, 0:512], cstb[0:1, CB_ONES:CB_ONES + 64],
                                             rl_lo[:], start=False, stop=True), "pe")
                            on("tensor", ft)
                            psum_w[4] = C["pe"]
                            def fs2(e):
                                wt(e, "scalar", ("pe", psum_w[4]))
                                fin(e.activation(rbc[:], PF[4][0:64, 0:512], AF.Copy), "act")
                            on("scalar", fs2)
                            bc_act = C["act"]
                            psum_r[4] = ("act", bc_act)
                            half = (hh % 2) * 64
                            def fv2(e, pav=pav, half=half, c=c, bc_act=bc_act):
                                wt(e, "vector", ("act", bc_act))
                                fin(e.tensor_tensor(OT[c][half:half + 64, :], pav[0:64, 0:512],
                                                    rbc[:], ALU.mult), "dve")
                            on("vector", fv2)
                        pav_r = ("dve", C["dve"])
                    ot_dve = ("dve", C["dve"])
                    psum_r[5] = pav_r
                    psum_r[6] = pav_r

                    wname = "wo" if flavor == "a" else "wproj"
                    bcol0 = bias_cols["bo" if flavor == "a" else "bproj"]
                    dT = Qb
                    for mo in range(8):
                        def ft(e, mo=mo):
                            wt(e, "tensor", ot_dve, psum_r[mo % 2], wgt_rdy)
                            mm_group(e, PF[mo % 2], wname, list(range(8)), mo * 128, OT)
                        on("tensor", ft)
                        psum_w[mo % 2] = C["pe"]
                        def fs(e, mo=mo):
                            wt(e, "scalar", ("pe", psum_w[mo % 2]))
                            fin(e.activation(dT[mo][:], PF[mo % 2][:, 0:512], AF.Identity,
                                             bias=cstf[:, bcol0 + mo:bcol0 + mo + 1]), "act")
                        on("scalar", fs)
                        psum_r[mo % 2] = ("act", C["act"])
                    dT_act = C["act"]
                    for j in range(4):
                        def ft(e, j=j):
                            wt(e, "tensor", ("act", dT_act), ST["tb_r"],
                               ("dve", ST["tb_resid"]))
                            for cq in range(8):
                                ins = e.transpose(TB[:, cq * 128:(cq + 1) * 128],
                                                  dT[cq][:, j * 128:(j + 1) * 128],
                                                  cstb[:, CB_ID:CB_ID + 128])
                            fin(ins, "pe")
                        on("tensor", ft)
                        tbw = C["pe"]
                        def fv(e, j=j, tbw=tbw):
                            wt(e, "vector", ("pe", tbw))
                            fin(e.scalar_tensor_tensor(xsl[j][:], TB[:], 1.0, xsl[j][:],
                                                       ALU.mult, ALU.add), "dve")
                        on("vector", fv)
                        ST["tb_resid"] = C["dve"]
                        ST["tb_r"] = ("dve", C["dve"])
                        res_dve = C["dve"]
                        def fst(e, j=j, tj=tiles[j], res_dve=res_dve):
                            wt(e, "sync", ("dve", res_dve), (f"dx{j}", C[f"dx{j}"]))
                            fin(e.dma_start(out=dst_dram[tj * 128:(tj + 1) * 128, :], in_=xsl[j][:]), f"dx{j}", 16)
                        on("sync", fst)
                        ST[f"xsl{j}"] = (f"dx{j}", C[f"dx{j}"])
                    ST["attn_pe_done"] = C["pe"]
                barrier()
            sc.close()
            barrier()

        def emit_moe(src_dram, dst_dram, sfx=""):
            sc = contextlib.ExitStack()
            lnf = sc.enter_context(nc.sbuf_tensor("lnf" + sfx, [128, DM], F32))
            lnTf = [sc.enter_context(nc.sbuf_tensor(f"lnTf{c}" + sfx, [128, 512], F32)) for c in range(8)]
            lgF = sc.enter_context(nc.sbuf_tensor("lgF" + sfx, [16, 512], F32))
            lgtok = sc.enter_context(nc.sbuf_tensor("lgtok" + sfx, [128, 64], F32))
            r16 = sc.enter_context(nc.sbuf_tensor("r16" + sfx, [128, 16], F32))
            r16b = sc.enter_context(nc.sbuf_tensor("r16b" + sfx, [128, 16], F32))
            dsp = sc.enter_context(nc.sbuf_tensor("dsp" + sfx, [128, 3 * DM], BF16))
            dispT = [sc.enter_context(nc.sbuf_tensor(f"dispT{c}" + sfx, [128, CLOC], BF16)) for c in range(8)]
            w1st = [sc.enter_context(nc.sbuf_tensor(f"w1st{i}" + sfx, [128, 8 * 512], BF16)) for i in range(3)]
            w2st = [sc.enter_context(nc.sbuf_tensor(f"w2st{i}" + sfx, [128, 32 * 256], BF16)) for i in range(2)]
            hbuf = sc.enter_context(nc.sbuf_tensor("hbuf" + sfx, [128, 32 * CLOC], BF16))
            eoT = [sc.enter_context(nc.sbuf_tensor(f"eoT{c}" + sfx, [128, CLOC], BF16)) for c in range(8)]
            eotok = sc.enter_context(nc.sbuf_tensor("eotok" + sfx, [128, DM], BF16))
            gbuf = [sc.enter_context(nc.sbuf_tensor(f"gbuf{i}" + sfx, [128, DM], BF16)) for i in range(4)]

            psum_r = {i: None for i in range(7)}
            psum_w = {i: 0 for i in range(7)}

            def f(e):
                fin(e.memset(runoff[:], 0.0), "dve")
            on("vector", f)

            r16_pe = 0
            for sg in range(8):
                tiles = [4 * sg + j for j in range(4)]
                lndve = []
                for j in range(4):
                    d, _ = emit_ln_one(src_dram, tiles[j], j, lnf, lnb, xsl)
                    lndve.append(d)
                    ST[f"xsl{j}"] = ("dve", d)
                    for half in range(2):
                        def ft(e, j=j, half=half, d=d):
                            wt(e, "tensor", ("dve", d), psum_r[4])
                            for b in range(4):
                                cch = half * 4 + b
                                ins = e.transpose(PF[4][:, b * 128:(b + 1) * 128],
                                                  lnf[:, cch * 128:(cch + 1) * 128],
                                                  cstf[:, CF_IDF:CF_IDF + 128])
                            fin(ins, "pe")
                        on("tensor", ft)
                        psum_w[4] = C["pe"]
                        def fs(e, j=j, half=half):
                            wt(e, "scalar", ("pe", psum_w[4]))
                            for b in range(4):
                                cch = half * 4 + b
                                ins = e.activation(lnTf[cch][:, j * 128:(j + 1) * 128],
                                                   PF[4][:, b * 128:(b + 1) * 128], AF.Copy)
                            fin(ins, "act")
                        on("scalar", fs)
                        psum_r[4] = ("act", C["act"])
                    ST["lnf_war"] = ("pe", psum_w[4])
                lnTf_act = C["act"]
                def ft(e):
                    wt(e, "tensor", ("act", lnTf_act), psum_r[0], ("act", gwT_act))
                    for cch in range(8):
                        ins = e.matmul(PF[0][0:16, 0:512], gwT[:, cch * 16:(cch + 1) * 16],
                                       lnTf[cch][:], start=(cch == 0), stop=(cch == 7))
                    fin(ins, "pe")
                on("tensor", ft)
                psum_w[0] = C["pe"]
                def fs(e):
                    wt(e, "scalar", ("pe", psum_w[0]))
                    fin(e.activation(lgF[:], PF[0][0:16, 0:512], AF.Copy), "act")
                on("scalar", fs)
                lgF_act = C["act"]
                psum_r[0] = ("act", lgF_act)
                def ft2(e):
                    wt(e, "tensor", ("act", lgF_act), psum_r[1])
                    for j in range(4):
                        ins = e.transpose(PF[1][:, j * 16:(j + 1) * 16],
                                          lgF[:, j * 128:(j + 1) * 128],
                                          cstf[0:16, CF_IDF:CF_IDF + 16])
                    fin(ins, "pe")
                on("tensor", ft2)
                psum_w[1] = C["pe"]
                def fs2(e):
                    wt(e, "scalar", ("pe", psum_w[1]))
                    fin(e.activation(lgtok[:], PF[1][:, 0:64], AF.Copy), "act")
                on("scalar", fs2)
                lg_act = C["act"]
                psum_r[1] = ("act", lg_act)

                for j in range(4):
                    tj = tiles[j]
                    lg = lgtok[:, j * 16:(j + 1) * 16]
                    def fv(e, lg=lg):
                        wt(e, "vector", ("act", lg_act))
                        fin(e.reduce_max(cols[:, 5:6], lg, axis=AX.X), "dve")
                        wt(e, "vector", ("dve", C["dve"]))
                        fin(e.tensor_scalar(cols[:, 6:7], cols[:, 5:6], -1.0, None, ALU.mult), "dve")
                    on("vector", fv)
                    negm_dve = C["dve"]
                    def fs3(e, lg=lg, r16_pe=r16_pe):
                        wt(e, "scalar", ("dve", negm_dve), ("pe", r16_pe))
                        fin(e.activation(r16[:], lg, AF.Exp, bias=cols[:, 6:7]), "act")
                    on("scalar", fs3)
                    exp_act = C["act"]
                    def fv2(e, lg=lg, tj=tj):
                        wt(e, "vector", ("act", exp_act))
                        fin(e.reduce_sum(cols[:, 7:8], r16[:], axis=AX.X), "dve")
                        wt(e, "vector", ("dve", C["dve"]))
                        fin(e.reciprocal(probb[:, tj:tj + 1], cols[:, 7:8]), "dve")
                        wt(e, "vector", ("dve", C["dve"]))
                        fin(e.tensor_scalar(r16[:], lg, cols[:, 5:6], None, ALU.is_equal), "dve")
                        wt(e, "vector", ("dve", C["dve"]))
                        fin(e.tensor_tensor(r16b[:], r16[:], cstf[:, CF_REV:CF_REV + 16], ALU.mult), "dve")
                        wt(e, "vector", ("dve", C["dve"]))
                        fin(e.reduce_max(cols[:, 8:9], r16b[:], axis=AX.X), "dve")
                        wt(e, "vector", ("dve", C["dve"]))
                        fin(e.tensor_scalar(eidxb[:, tj:tj + 1], cols[:, 8:9], -1.0, 99.0,
                                            ALU.mult, ALU.add), "dve")
                        wt(e, "vector", ("dve", C["dve"]))
                        fin(e.tensor_scalar(r16[:], cstf[:, CF_IOTA:CF_IOTA + 16],
                                            eidxb[:, tj:tj + 1], None, ALU.is_equal), "dve")
                    on("vector", fv2)
                    oh_dve = C["dve"]
                    def ft3(e):
                        wt(e, "tensor", ("dve", oh_dve), psum_r[2])
                        e.matmul(PF[2][:, 0:16], cstf[:, CF_SUT:CF_SUT + 128], r16[:],
                                 start=True, stop=False)
                        e.matmul(PF[2][:, 0:16], cstf[0:1, CF_ONES:CF_ONES + 128], runoff[:],
                                 start=False, stop=True)
                        fin(e.matmul(PF[2][0:1, 16:32], cstf[:, CF_ONES:CF_ONES + 1], r16[:],
                                     start=True, stop=True), "pe")
                    on("tensor", ft3)
                    psum_w[2] = C["pe"]
                    r16_pe = C["pe"]
                    def fv3(e, tj=tj):
                        wt(e, "vector", ("pe", psum_w[2]))
                        fin(e.tensor_tensor(r16b[:], PF[2][:, 0:16], r16[:], ALU.mult), "dve")
                        wt(e, "vector", ("dve", C["dve"]))
                        fin(e.reduce_sum(rankb[:, tj:tj + 1], r16b[:], axis=AX.X), "dve")
                        wt(e, "vector", ("dve", C["dve"]))
                        fin(e.tensor_tensor(runoff[:], runoff[:], PF[2][0:1, 16:32], ALU.add), "dve")
                        e.scalar_tensor_tensor(cols[:, 9:10], eidxb[:, tj:tj + 1], float(CLOC),
                                               rankb[:, tj:tj + 1], ALU.mult, ALU.add)
                        fin(e.tensor_scalar(cols[:, 10:11], rankb[:, tj:tj + 1], CLOC - 0.5, None, ALU.is_gt), "dve", 2)
                        wt(e, "vector", ("dve", C["dve"]))
                        fin(e.tensor_scalar(cols[:, 11:12], cols[:, 10:11], -1.0, 1.0, ALU.mult, ALU.add), "dve")
                        wt(e, "vector", ("dve", C["dve"]))
                        fin(e.tensor_tensor(cols[:, 12:13], cols[:, 9:10], cols[:, 11:12], ALU.mult), "dve")
                        wt(e, "vector", ("dve", C["dve"]))
                        fin(e.scalar_tensor_tensor(cols[:, 13:14], cols[:, 10:11], float(TRASH_W),
                                                   cols[:, 12:13], ALU.mult, ALU.add), "dve")
                        wt(e, "vector", ("dve", C["dve"]))
                        fin(e.tensor_copy(idxb[:, tj:tj + 1], cols[:, 13:14]), "dve")
                    on("vector", fv3)
                    psum_r[2] = ("dve", C["dve"])
                    idx_dve = C["dve"]
                    def fg(e, j=j, tj=tj, idx_dve=idx_dve):
                        wt(e, "gpsimd", ("dve", idx_dve), (f"dsct{j}", C[f"dsct{j}"]))
                        fin(e.indirect_dma_start(
                            disp, bass.IndirectOffsetOnAxis(ap=idxb[:, tj:tj + 1], axis=0),
                            lnb[j][:], None), f"dsct{j}", 16)
                    on("gpsimd", fg)
                    ST[f"lnb{j}"] = (f"dsct{j}", C[f"dsct{j}"])

            scatter_done = [(f"dsct{j}", C[f"dsct{j}"]) for j in range(4)]
            ro_dve = C["dve"]
            def f(e):
                wt(e, "sync", ("dve", ro_dve), ("dcc", C["dcc"]))
                fin(e.dma_start(out=ccin.ap()[0:1, 0:16], in_=runoff[:]), "dcc", 16)
            on("sync", f)
            ccin_dsy = C["dcc"]
            import kernel as _kmod
            if getattr(_kmod, "SIM_NOCC", False) or SIM_NOCC:
                def fg2(e):
                    wt(e, "vector", ("dcc", ccin_dsy))
                    fin(e.memset(sg8[:], 0.0), "dve")
                on("vector", fg2)
                sg_dgp = ("dve", C["dve"])
            else:
                def fg2(e):
                    wt(e, "gpsimd", ("dcc", ccin_dsy))
                    fin(e.collective_compute("AllGather", ALU.bypass,
                                             replica_groups=[list(range(8))],
                                             ins=[ccin.ap().opt()], outs=[ccout.ap().opt()]), "cc")
                    wt(e, "gpsimd", ("cc", C["cc"]))
                    fin(e.dma_start(out=sg8[:], in_=ccout.ap()[:, 0:16]), "dcc", 16)
                on("gpsimd", fg2)
                sg_dgp = ("dcc", C["dcc"])
            def ft4(e):
                wt(e, "tensor", sg_dgp, psum_r[3], ("dcst", start_dsy))
                fin(e.matmul(PF[3][0:1, 0:16], cmask[:], sg8[:], start=True, stop=True), "pe")
            on("tensor", ft4)
            psum_w[3] = C["pe"]
            def fs4(e):
                wt(e, "scalar", ("pe", psum_w[3]))
                fin(e.activation(gbase[:], PF[3][0:1, 0:16], AF.Copy), "act")
            on("scalar", fs4)
            gb_act = C["act"]
            def ft5(e):
                wt(e, "tensor", ("act", gb_act))
                fin(e.matmul(PF[3][:, 16:32], cstf[0:1, CF_ONES:CF_ONES + 128], gbase[:],
                             start=True, stop=True), "pe")
            on("tensor", ft5)
            psum_w[3] = C["pe"]
            def fs5(e):
                wt(e, "scalar", ("pe", psum_w[3]))
                fin(e.activation(gbb[:], PF[3][:, 16:32], AF.Copy), "act")
            on("scalar", fs5)
            gbb_act = C["act"]
            psum_r[3] = ("act", gbb_act)

            for tj in range(NTILE):
                def fvw(e, tj=tj):
                    wt(e, "vector", ("act", gbb_act))
                    fin(e.tensor_scalar(r16[:], cstf[:, CF_IOTA:CF_IOTA + 16], eidxb[:, tj:tj + 1],
                                        None, ALU.is_equal), "dve")
                    wt(e, "vector", ("dve", C["dve"]))
                    fin(e.tensor_tensor(r16b[:], r16[:], gbb[:], ALU.mult), "dve")
                    wt(e, "vector", ("dve", C["dve"]))
                    fin(e.reduce_sum(cols[:, 9:10], r16b[:], axis=AX.X), "dve")
                    wt(e, "vector", ("dve", C["dve"]))
                    fin(e.tensor_tensor(cols[:, 10:11], cols[:, 9:10], rankb[:, tj:tj + 1], ALU.add), "dve")
                    wt(e, "vector", ("dve", C["dve"]))
                    fin(e.tensor_scalar(cols[:, 11:12], cols[:, 10:11], CAP - 0.5, None, ALU.is_lt), "dve")
                    wt(e, "vector", ("dve", C["dve"]))
                    fin(e.tensor_tensor(wcomb[:, tj:tj + 1], cols[:, 11:12], probb[:, tj:tj + 1], ALU.mult), "dve")
                on("vector", fvw)
            wcomb_dve = C["dve"]

            w1_pe = {0: 0, 1: 0, 2: 0}
            w2_pe = {0: 0, 1: 0}
            dsp_act = 0
            eotok_st = 0
            for ex in range(NEXP):
                def f(e, ex=ex, dsp_act=dsp_act):
                    wt(e, "sync", ("act", dsp_act), ("ddsp", C["ddsp"]), *scatter_done)
                    src = disp[ex * CLOC:(ex + 1) * CLOC, :].rearrange("(t p) d -> p t d", p=128)
                    fin(e.dma_start(out=dsp[:].rearrange("p (t d) -> p t d", t=3), in_=src), "ddsp", 16)
                on("sync", f)
                dsp_dsy = C["ddsp"]
                for cch in range(8):
                    def ft(e, cch=cch, dsp_dsy=dsp_dsy):
                        wt(e, "tensor", ("ddsp", dsp_dsy), ST["tb_r"])
                        for t in range(3):
                            ins = e.transpose(TB[:, t * 128:(t + 1) * 128],
                                              dsp[:, t * DM + cch * 128:t * DM + (cch + 1) * 128],
                                              cstb[:, CB_ID:CB_ID + 128])
                        fin(ins, "pe")
                    on("tensor", ft)
                    tbw = C["pe"]
                    def fs(e, cch=cch, tbw=tbw):
                        wt(e, "scalar", ("pe", tbw))
                        fin(e.activation(dispT[cch][:], TB[:, 0:CLOC], AF.Copy), "act")
                    on("scalar", fs)
                    ST["tb_r"] = ("act", C["act"])
                dispT_act = C["act"]
                dsp_act = dispT_act

                for fg_ in range(8):
                    slot = fg_ % 3
                    def fgl(e, ex=ex, fg_=fg_, slot=slot):
                        wt(e, "gpsimd", ("pe", w1_pe[slot]), (f"dw1{slot}", C[f"dw1{slot}"]))
                        src = wdr["w1"][ex].rearrange("(kc p) f -> p kc f", p=128)[:, :, fg_ * 512:(fg_ + 1) * 512]
                        fin(e.dma_start(out=w1st[slot][:].rearrange("p (kc f) -> p kc f", kc=8),
                                        in_=src), f"dw1{slot}", 16)
                    on("gpsimd", fgl)
                    w1_dgp = (f"dw1{slot}", C[f"dw1{slot}"])
                    for fc4 in range(4):
                        ps = fc4 % 2
                        def ft(e, fc4=fc4, slot=slot, ps=ps, w1_dgp=w1_dgp, dispT_act=dispT_act):
                            wt(e, "tensor", w1_dgp, ("act", dispT_act), psum_r[ps])
                            for kc in range(8):
                                ins = e.matmul(PF[ps][:, 0:CLOC],
                                               w1st[slot][:, kc * 512 + fc4 * 128:kc * 512 + fc4 * 128 + 128],
                                               dispT[kc][:], start=(kc == 0), stop=(kc == 7))
                            fin(ins, "pe")
                        on("tensor", ft)
                        psum_w[ps] = C["pe"]
                        w1_pe[slot] = C["pe"]
                        def fs(e, fg_=fg_, fc4=fc4, ps=ps):
                            wt(e, "scalar", ("pe", psum_w[ps]))
                            fin(e.activation(hbuf[:, (fg_ * 4 + fc4) * CLOC:(fg_ * 4 + fc4 + 1) * CLOC],
                                             PF[ps][:, 0:CLOC], AF.Gelu_apprx_tanh), "act")
                        on("scalar", fs)
                        psum_r[ps] = ("act", C["act"])
                h_act = C["act"]
                for dg in range(4):
                    slot = dg % 2
                    def fgl(e, ex=ex, dg=dg, slot=slot):
                        wt(e, "gpsimd", ("pe", w2_pe[slot]), (f"dw2{slot}", C[f"dw2{slot}"]))
                        src = wdr["w2"][ex].rearrange("(kf p) d -> p kf d", p=128)[:, :, dg * 256:(dg + 1) * 256]
                        fin(e.dma_start(out=w2st[slot][:].rearrange("p (kf d) -> p kf d", kf=32),
                                        in_=src), f"dw2{slot}", 16)
                    on("gpsimd", fgl)
                    w2_dgp = (f"dw2{slot}", C[f"dw2{slot}"])
                    for dc2 in range(2):
                        ps = 2 + dc2 % 2
                        def ft(e, dc2=dc2, slot=slot, ps=ps, w2_dgp=w2_dgp, h_act=h_act):
                            wt(e, "tensor", w2_dgp, ("act", h_act), psum_r[ps])
                            for kf in range(32):
                                ins = e.matmul(PF[ps][:, 0:CLOC],
                                               w2st[slot][:, kf * 256 + dc2 * 128:kf * 256 + dc2 * 128 + 128],
                                               hbuf[:, kf * CLOC:(kf + 1) * CLOC],
                                               start=(kf == 0), stop=(kf == 31))
                            fin(ins, "pe")
                        on("tensor", ft)
                        psum_w[ps] = C["pe"]
                        w2_pe[slot] = C["pe"]
                        def fs(e, dg=dg, dc2=dc2, ps=ps):
                            wt(e, "scalar", ("pe", psum_w[ps]))
                            fin(e.activation(eoT[dg * 2 + dc2][:], PF[ps][:, 0:CLOC], AF.Copy), "act")
                        on("scalar", fs)
                        psum_r[ps] = ("act", C["act"])
                eo_act = C["act"]
                for t in range(3):
                    def ft(e, t=t, eo_act=eo_act):
                        wt(e, "tensor", ("act", eo_act), ST["tb_r"])
                        for cch in range(8):
                            ins = e.transpose(TB[:, cch * 128:(cch + 1) * 128],
                                              eoT[cch][:, t * 128:(t + 1) * 128],
                                              cstb[:, CB_ID:CB_ID + 128])
                        fin(ins, "pe")
                    on("tensor", ft)
                    tbw = C["pe"]
                    def fs(e, t=t, tbw=tbw, eotok_st=eotok_st):
                        wt(e, "scalar", ("pe", tbw), ("deot", eotok_st))
                        fin(e.activation(eotok[:], TB[:], AF.Copy), "act")
                    on("scalar", fs)
                    ST["tb_r"] = ("act", C["act"])
                    eot_act = C["act"]
                    def fst(e, ex=ex, t=t, eot_act=eot_act):
                        wt(e, "sync", ("act", eot_act), ("deot", C["deot"]))
                        fin(e.dma_start(out=expout[ex * CLOC + t * 128:ex * CLOC + (t + 1) * 128, :],
                                        in_=eotok[:]), "deot", 16)
                    on("sync", fst)
                    eotok_st = C["deot"]
            eostore_dsy = C["deot"]

            gb_used = {i: 0 for i in range(4)}
            for tj in range(NTILE):
                gs = tj % 4
                gslot = gbuf[gs]
                def fg(e, tj=tj, gs=gs, gslot=gslot):
                    wt(e, "gpsimd", ("deot", eostore_dsy), ("dve", gb_used[gs]),
                       (f"dgx{gs}", C[f"dgx{gs}"]))
                    fin(e.indirect_dma_start(
                        gslot[:], None, expout,
                        bass.IndirectOffsetOnAxis(ap=idxb[:, tj:tj + 1], axis=0)), f"dgx{gs}", 16)
                on("gpsimd", fg)
                g_dgp = (f"dgx{gs}", C[f"dgx{gs}"])
                xslot = xsl[tj % 4]
                def f(e, tj=tj, xslot=xslot):
                    wt(e, "sync", ST[f"xsl{tj % 4}"], (f"dx{tj % 4}", C[f"dx{tj % 4}"]))
                    fin(e.dma_start(out=xslot[:], in_=src_dram[tj * 128:(tj + 1) * 128, :]), f"dx{tj % 4}", 16)
                on("sync", f)
                x_dsy = (f"dx{tj % 4}", C[f"dx{tj % 4}"])
                def fv(e, tj=tj, g_dgp=g_dgp, x_dsy=x_dsy, gslot=gslot, xslot=xslot):
                    wt(e, "vector", g_dgp, x_dsy, ("dve", wcomb_dve))
                    fin(e.scalar_tensor_tensor(xslot[:], gslot[:], wcomb[:, tj:tj + 1], xslot[:],
                                               ALU.mult, ALU.add), "dve")
                on("vector", fv)
                gb_used[gs] = C["dve"]
                res_dve = C["dve"]
                def fst(e, tj=tj, xslot=xslot, res_dve=res_dve):
                    wt(e, "sync", ("dve", res_dve), (f"dx{tj % 4}", C[f"dx{tj % 4}"]))
                    fin(e.dma_start(out=dst_dram[tj * 128:(tj + 1) * 128, :], in_=xslot[:]), f"dx{tj % 4}", 16)
                on("sync", fst)
                ST[f"xsl{tj % 4}"] = (f"dx{tj % 4}", C[f"dx{tj % 4}"])
            sc.close()
            barrier()

        def load_arena_a():
            def f(e):
                for nm, base, kch in [("wq", 0, 8), ("wkv", 2048, 8), ("wqc", 4096, 2),
                                      ("wkc", 5888, 2), ("wqr", 7680, 8), ("wkr", 8704, 8),
                                      ("wv", 9728, 2)]:
                    ncols = (wdr[nm].shape[1] if hasattr(wdr[nm], "shape") else 0)
                    s2 = wdr[nm].rearrange("(kc p) n -> p kc n", p=128)
                    n = s2.shape[2]
                    dst = arena[:, base:base + kch * n].rearrange("p (kc n) -> p kc n", kc=kch)
                    fin(e.dma_start(out=dst, in_=s2), "dwa0", 16)
            on("gpsimd", f)
            ST["arena_rdy"] = ("dwa0", C["dwa0"])

        def load_arena_b():
            def f(e):
                s2 = wdr["wqkv"].rearrange("(kc p) n -> p kc n", p=128)
                dst = arena[:].rearrange("p (kc n) -> p kc n", kc=8)
                fin(e.dma_start(out=dst, in_=s2[:, :, 0:2048]), "dwa0", 16)
            on("gpsimd", f)
            ST["arena_rdy"] = ("dwa0", C["dwa0"])

        if n_sub >= 1:
            CURSCOPE[0] = "attnA"
            load_arena_a()
            emit_attn("a", xin, out if n_sub == 1 else xres)
        if n_sub >= 2:
            CURSCOPE[0] = "moe0"
            load_arena_b()
            emit_moe(xres, out if n_sub == 2 else xres, "_m0")
        if n_sub >= 3:
            CURSCOPE[0] = "attnB"
            emit_attn("b", xres, out if n_sub == 3 else xres)
        if n_sub >= 4:
            CURSCOPE[0] = "moe1"
            emit_moe(xres, out, "_m1")
        CURSCOPE[0] = "tail"

        snap = dict(C)
        def f(e):
            wt(e, "vector", *snap.items())
        on("vector", f)
        flush_sections()

    return nc


_CACHE = {}

def _get_program(n_sub=4):
    if n_sub not in _CACHE:
        _CACHE[n_sub] = build_program(n_sub)
    return _CACHE[n_sub]


def make_in_maps(inputs):
    x = np.ascontiguousarray(np.asarray(inputs["L0_attn_a_x"], dtype=np.float32))
    perm = _perm_qk_cols()
    wqkv = np.ascontiguousarray(np.asarray(inputs["L0_attn_b_W_qkv"])[:, perm])
    bqkv = np.ascontiguousarray(np.asarray(inputs["L0_attn_b_b_qkv"])[perm])
    bias_map = [
        ("bq", np.asarray(inputs["L0_attn_a_bq"], dtype=np.float32)),
        ("bqc", np.asarray(inputs["L0_attn_a_bqc"], dtype=np.float32)),
        ("bqr", np.asarray(inputs["L0_attn_a_bqr"], dtype=np.float32)),
        ("bkv", np.asarray(inputs["L0_attn_a_bkv"], dtype=np.float32)),
        ("bkr", np.asarray(inputs["L0_attn_a_bkr"], dtype=np.float32)),
        ("bkc", np.asarray(inputs["L0_attn_a_bkc"], dtype=np.float32)),
        ("bv", np.asarray(inputs["L0_attn_a_bv"], dtype=np.float32)),
        ("bo", np.asarray(inputs["L0_attn_a_bo"], dtype=np.float32)),
        ("bqkv", bqkv.astype(np.float32)),
        ("bproj", np.asarray(inputs["L0_attn_b_b_proj"], dtype=np.float32)),
    ]
    cstf, cstb = _build_consts(bias_map)
    C1a, C2a, C1b, C2b = _rope_tables()
    rope = np.ascontiguousarray(np.concatenate([C1a, C2a, C1b, C2b], 1))
    weights = {
        "wq": inputs["L0_attn_a_wq"], "wqc": inputs["L0_attn_a_wqc"],
        "wqr": inputs["L0_attn_a_wqr"], "wkv": inputs["L0_attn_a_wkv"],
        "wkr": inputs["L0_attn_a_wkr"], "wkc": inputs["L0_attn_a_wkc"],
        "wv": inputs["L0_attn_a_wv"], "wo": inputs["L0_attn_a_wo"],
        "wqkv": wqkv, "wproj": inputs["L0_attn_b_W_proj"],
        "gw": inputs["L0_ffn_gate_w"], "w1": inputs["L0_ffn_w1"],
        "w2": inputs["L0_ffn_w2"],
    }
    weights = {k: np.ascontiguousarray(np.asarray(v, dtype=np.float32)) for k, v in weights.items()}
    in_maps = []
    for k in range(NCORES):
        cm = np.zeros((8, 1), dtype=np.float32)
        cm[:k, 0] = 1.0
        m = dict(weights)
        m["xin"] = np.ascontiguousarray(x[2 * k:2 * k + 2].reshape(TOK, DM))
        m["cstf"] = cstf
        m["cstb"] = cstb
        m["rope"] = rope
        m["cmask"] = cm
        in_maps.append(m)
    return in_maps


def kernel(**inputs):
    nc = _get_program(4)
    in_maps = make_in_maps(inputs)
    res = run_bass_kernel_spmd(nc, in_maps, core_ids=list(range(NCORES)))
    parts = [r["out"].reshape(NSEQ, SEQ, DM) for r in res.results]
    return np.ascontiguousarray(np.concatenate(parts, 0)).reshape(16, SEQ, DM)
